# version 08 (final)
# speedup vs baseline: 1.1847x; 1.0330x over previous
import numpy as np
from contextlib import ExitStack

import concourse.bass as bass
import concourse.mybir as mybir
from concourse.bass_utils import run_bass_kernel_spmd

F32 = mybir.dt.float32
BF16 = mybir.dt.bfloat16
FP16 = mybir.dt.float16
AF = mybir.ActivationFunctionType
ALU = mybir.AluOpType
AX = mybir.AxisListType

D = 1024
HID = 4096
T = 2048
NT = 16
S = 4096
NH = 8
DH = 64
NC = 8
ROFF = 1536.0
EPS = 1e-5

KBASE = 0
VBASE = 16384


class MK:
    def __init__(self, parts, plan=None, dbg=()):
        self.parts = parts
        self.dbg = set(dbg)
        self.plan = plan
        self.nc = bass.Bass("TRN2", target_bir_lowering=False, debug=False)
        self.cnt = {}
        self.sems = {}
        self.ev = {}
        self.slots = {}
        self.uid = 0

    def mksem(self, E, name):
        self.sems[name] = E(self.nc.semaphore(name))
        self.cnt[name] = 0

    def bump(self, inst, name, k):
        self.cnt[name] += k
        inst.then_inc(self.sems[name], k)
        return inst

    def rec(self, tag, name):
        self.ev[tag] = (name, self.cnt[name])

    def wtag(self, eng, tag):
        if self.plan is not None:
            name, v = self.plan[tag]
        elif tag in self.ev:
            name, v = self.ev[tag]
        else:
            return
        if v > 0:
            eng.wait_ge(self.sems[name], v)

    def wall(self, eng, name):
        if self.cnt[name] > 0:
            eng.wait_ge(self.sems[name], self.cnt[name])

    def selfsem(self, eng, name):
        self.wall(eng, name)

    def slot(self, name):
        if name not in self.slots:
            self.slots[name] = len(self.slots)
            assert len(self.slots) <= 64
        i = self.slots[name]
        return self.sc.ap()[:, i:i + 1]

    def statc(self, i, n=1):
        return self.stats.ap()[:, i:i + n]

    def xt(self, t):
        return self.A.ap()[:, t * 1024:(t + 1) * 1024]

    def vsl(self, tt, h=None):
        if tt == 31:
            a, b0 = self.ctx_x.ap(), 0
        else:
            a, b0 = self.A.ap().bitcast(BF16), VBASE + tt * 520
        if h is None:
            return a[:, b0:b0 + 520]
        return a[:, b0 + h * 65: b0 + h * 65 + 65]

    def ksl(self, pt, col0, ncol):
        base = KBASE + pt * 4096 + col0
        return self.A.ap().bitcast(BF16)[:, base:base + ncol]

    def qsl(self, pt, col0, ncol):
        return self.B.ap()[:, pt * 4096 + col0: pt * 4096 + col0 + ncol]

    def xnT(self, it, col0, ncol):
        return self.C.ap()[:, it * 4096 + col0: it * 4096 + col0 + ncol]

    def ctxsl(self, pt, col0, ncol):
        return self.C.ap()[:, pt * 4096 + col0: pt * 4096 + col0 + ncol]

    def wosl(self, vt, col0, ncol):
        base = 4 * 4096 + vt * 1024 + col0
        return self.C.ap()[:, base:base + ncol]

    def wsrc(self, k):
        if k < 32:
            return self.w1_in[k * 128:(k + 1) * 128, :]
        rb, cc = (k - 32) // 4, (k - 32) % 4
        return self.w2_in[rb * 128:(rb + 1) * 128, cc * 1024:(cc + 1) * 1024]

    def wdst(self, k):
        if k < 32:
            return self.wq1s[k * 128:(k + 1) * 128, :]
        rb, cc = (k - 32) // 4, (k - 32) % 4
        return self.wq2s[rb * 128:(rb + 1) * 128, cc * 1024:(cc + 1) * 1024]

    def ewf(self, par, ncol=1024):
        return self.Ew.ap().bitcast(F32)[:, par * 1024: par * 1024 + ncol]

    def ewb(self, par, ncol=2048):
        return self.Ew.ap()[:, par * 2048: par * 2048 + ncol]

    def ewh(self, par, ncol=2048):
        return self.Ew.ap().bitcast(FP16)[:, par * 2048: par * 2048 + ncol]

    @property
    def stgu(self):
        return self.stg1

    def lrow_ap(self):
        return self.oT.ap().bitcast(BF16)[:, 0:512]

    def linv_ap(self):
        return self.oT.ap()[:, 256:768]

    def xqbuf(self, par):
        return self.expb.ap()[:, par * 1024:(par + 1) * 1024]

    def b2v(self):
        return self.B.ap().bitcast(F32)[:, 0:1024]

    def dstg(self, par, col0=0, ncol=4096):
        return self.Dg.ap()[:, par * 4096 + col0: par * 4096 + col0 + ncol]

    def cwf(self, m, ncol=1024):
        return self.C.ap().bitcast(F32)[:, 2048 + m * 1024:
                                        2048 + m * 1024 + ncol]

    def cstgh(self, par, ncol=2048):
        return self.C.ap().bitcast(FP16)[:, par * 2048: par * 2048 + ncol]

    def dgf(self, par, ncol=1024):
        return self.Dg.ap().bitcast(F32)[:, par * 1024: par * 1024 + ncol]

    def dstgh(self, par, col0=0, ncol=4096):
        return self.Dg.ap().bitcast(FP16)[:, par * 4096 + col0:
                                          par * 4096 + col0 + ncol]

    def build(self):
        nc = self.nc
        self.x_in = nc.dram_tensor("x", [T, D], F32, kind="ExternalInput").ap()
        self.w1_in = nc.dram_tensor("w1", [HID, D], F32, kind="ExternalInput").ap()
        self.b1_in = nc.dram_tensor("b1", [HID], F32, kind="ExternalInput").ap()
        self.w2_in = nc.dram_tensor("w2", [D, HID], F32, kind="ExternalInput").ap()
        self.b2_in = nc.dram_tensor("b2", [D], F32, kind="ExternalInput").ap()
        self.wq_in = nc.dram_tensor("wqm", [NH * DH, D], F32, kind="ExternalInput").ap()
        self.wk_in = nc.dram_tensor("wkm", [NH * DH, D], F32, kind="ExternalInput").ap()
        self.wv_in = nc.dram_tensor("wvm", [NH * DH, D], F32, kind="ExternalInput").ap()
        self.wo_in = nc.dram_tensor("wom", [D, NH * DH], F32, kind="ExternalInput").ap()
        self.y_out = nc.dram_tensor("y", [T, D], F32, kind="ExternalOutput").ap()
        self.wq1s = nc.dram_tensor("wq1s", [HID, D], BF16).ap()
        self.wq2s = nc.dram_tensor("wq2s", [D, HID], BF16).ap()
        self.wabq = nc.dram_tensor("wabq", [NH * DH, D], BF16).ap()
        self.wabk = nc.dram_tensor("wabk", [NH * DH, D], BF16).ap()
        self.wabv = nc.dram_tensor("wabv", [NH * DH, D], BF16).ap()
        self.wabo = nc.dram_tensor("wabo", [D, NH * DH], BF16).ap()
        self.h_sp = nc.dram_tensor("h_sp", [HID, T], FP16).ap()
        self.x_sp = nc.dram_tensor("x_sp", [T, D], F32).ap()
        self.ag_i = nc.dram_tensor("ag_i", [D, T], BF16)
        self.ag_o = nc.dram_tensor("ag_o", [2, D, T], BF16)
        self.rs_i = nc.dram_tensor("rs_i", [S, D], F32)
        self.rs_o = nc.dram_tensor("rs_o", [T, D], F32)
        self.ar_i = [nc.dram_tensor("ar_i%d" % k, [2], F32) for k in range(4)]
        self.ar_o = [nc.dram_tensor("ar_o%d" % k, [2], F32) for k in range(4)]
        self.scw = nc.dram_tensor("scw", [4], F32)
        self.dout = {}
        for name, shape, dt in (
                ("d_sc", [128, 48], F32), ("d_stats", [128, 128], F32),
                ("d_wq1", [HID, D], BF16), ("d_wq2", [D, HID], BF16),
                ("d_xqT", [128, 16384], BF16), ("d_h", [HID, T], FP16),
                ("d_x1", [T, D], F32), ("d_xnT", [128, 32768], BF16),
                ("d_kq", [128, 32768], BF16), ("d_ctx", [128, 16384], BF16),
                ("d_rs", [T, D], F32), ("d_sc2", [128, 48], F32), ("d_b1", [128, 32], F32), ("d_b2", [128, 8], F32)):
            if name in self.dbg:
                self.dout[name] = nc.dram_tensor(name, shape, dt,
                                                 kind="ExternalOutput").ap()

        es = ExitStack()
        with es:
            E = es.enter_context
            self.A = E(nc.sbuf_tensor("A", [128, 16384], F32))
            self.B = E(nc.sbuf_tensor("B", [128, 16384], BF16))
            self.C = E(nc.sbuf_tensor("C", [128, 32768], BF16))
            self.Dg = E(nc.sbuf_tensor("Dg", [128, 2 * 4096], BF16))
            self.Ew = E(nc.sbuf_tensor("Ew", [128, 2 * 2048], BF16))
            self.stats = E(nc.sbuf_tensor("stats", [128, 128], F32))
            self.sc = E(nc.sbuf_tensor("sc", [128, 48], F32))
            self.b1c = E(nc.sbuf_tensor("b1c", [128, 32], F32))
            self.b2c = E(nc.sbuf_tensor("b2c", [128, 8], F32))
            self.identb = E(nc.sbuf_tensor("identb", [128, 128], BF16))
            self.identf = E(nc.sbuf_tensor("identf", [128, 128], F32))
            self.masks = E(nc.sbuf_tensor("masks", [128, 4 * 512], BF16))
            self.onesb = E(nc.sbuf_tensor("onesb", [128, 128], BF16))
            self.stg1 = E(nc.sbuf_tensor("stg1", [128, 1024], F32))
            self.stgh = E(nc.sbuf_tensor("stgh", [128, 1024], FP16))
            self.expb = E(nc.sbuf_tensor("expb", [128, 4 * 512], BF16))
            self.ctx_x = E(nc.sbuf_tensor("ctx_x", [128, 640], BF16))
            self.oT = E(nc.sbuf_tensor("oT", [128, 2 * 512], F32))
            self.pbt01 = E(nc.psum_tensor("pbt01", [128, 1024], F32))
            self.pbt23 = E(nc.psum_tensor("pbt23", [128, 1024], F32))
            _singles = [E(nc.psum_tensor("pb%d" % i, [128, 512], F32))
                        for i in range(4, 8)]

            class _Slot:
                def __init__(s, t, off):
                    s.t, s.off = t, off

                def ap(s):
                    if s.off is None:
                        return s.t.ap()
                    return s.t.ap()[:, s.off:s.off + 512]
            self.pb = [_Slot(self.pbt01, 0), _Slot(self.pbt01, 512),
                       _Slot(self.pbt23, 0), _Slot(self.pbt23, 512)] + \
                      [_Slot(t, None) for t in _singles]
            for s_ in ("dS", "dG", "dA", "vV", "sS", "tT", "gG", "cC",
                       "dX", "dW0", "dW1", "dW2", "dW3", "dT0", "dT1",
                       "dH0", "dH1", "dP0", "dP1", "dR0", "dR1", "dB2"):
                self.mksem(E, s_)
            for i in range(4):
                self.slot("bc%d" % i)

            self.ph_consts()
            self.ph_prep()
            self.dump("d_sc", src_ap=self.sc.ap())
            self.dump("d_b1", src_ap=self.b1c.ap())
            self.dump("d_b2", src_ap=self.b2c.ap())
            self.dump("d_stats", src_ap=self.stats.ap())
            self.dump("d_wq1", dram_src=self.wq1s)
            self.dump("d_wq2", dram_src=self.wq2s)
            if "ffn1" in self.parts:
                self.ph_ffn(0, "q1", "q2")
                self.dump("d_sc2", src_ap=self.sc.ap())
                self.dump("d_h", dram_src=self.h_sp)
                if "attn" not in self.parts:
                    self.dump("d_kq", src_ap=self.C.ap())
                    self.dump("d_ctx", src_ap=self.B.ap())
                if "d_x1" in self.dout:
                    with nc.Block() as blk:
                        def fx1(sy):
                            for t in range(NT):
                                self.bump(sy.dma_start(
                                    out=self.dout["d_x1"][t * 128:(t + 1) * 128, :],
                                    in_=self.xt(t)), "dS", 16)
                            self.wall(sy, "dS")
                        blk.sync(fx1)
            if "attn" in self.parts:
                self.ph_attn()
            if "ffn2" in self.parts:
                self.ph_ffn(2, "q3", "q4")
            self.ph_store()
        return self.nc

    def dump(self, name, src_ap=None, dram_src=None, rows=128):
        if name not in self.dout:
            return
        c = self
        with self.nc.Block() as blk:
            def fsync(sy):
                if dram_src is not None:
                    c.bump(sy.dma_start(out=self.dout[name], in_=dram_src),
                           "dS", 16)
                else:
                    c.bump(sy.dma_start(out=self.dout[name][0:rows, :],
                                        in_=src_ap), "dS", 16)
                c.wall(sy, "dS")
            blk.sync(fsync)

    def ph_consts(self):
        c = self

        def g0(g):
            g.memset(c.identb[:], 0.0)
            g.affine_select(out=c.identb[:], in_=c.identb[:],
                            compare_op=ALU.not_equal, fill=1.0, base=0,
                            pattern=[[-1, 128]], channel_multiplier=1)
            g.memset(c.identf[:], 0.0)
            g.affine_select(out=c.identf[:], in_=c.identf[:],
                            compare_op=ALU.not_equal, fill=1.0, base=0,
                            pattern=[[-1, 128]], channel_multiplier=1)
            g.memset(c.onesb[:], 1.0)
            g.memset(c.slot("c_invD"), 1.0 / D)
            g.memset(c.slot("c_neg_invD"), -1.0 / D)
            g.memset(c.slot("c_eps"), EPS)
            g.memset(c.slot("c_roff"), ROFF)
            g.memset(c.slot("c_exp8"), 0.125)
            g.memset(c.slot("c_neg1"), -1.0)
            for ks in range(4):
                m = c.masks.ap()[:, ks * 512:(ks + 1) * 512]
                g.memset(m, 1.0)
                g.affine_select(out=m, in_=m, compare_op=ALU.is_ge, fill=0.0,
                                base=-(ks * 128), pattern=[[1, 512]],
                                channel_multiplier=-1)
            c.bump(g.engine_nop(), "gG", 1)
            c.rec("consts", "gG")

        with self.nc.Block() as blk:
            blk.gpsimd(g0)

    def ph_prep(self):
        c = self
        nc = self.nc
        NW = 64

        with nc.Block() as blk:
            def fsync(sy):
                for t in range(NT):
                    c.bump(sy.dma_start(out=c.xt(t),
                                        in_=c.x_in[t * 128:(t + 1) * 128, :]),
                           "dX", 16)
                c.rec("xld_all", "dX")
                for k in range(NW):
                    src = c.wsrc(k)
                    if k >= 4:
                        c.wtag(sy, "wmax_%d" % (k - 4))
                    c.bump(sy.dma_start(out=c.dgf(k % 4), in_=src),
                           "dW%d" % (k % 4), 16)
                    c.rec("wld_%d" % k, "dW%d" % (k % 4))
                c.bump(sy.dma_start(out=c.oT[0:32, 0:128],
                                    in_=c.b1_in.rearrange("(a b) -> a b", b=128)),
                       "dX", 16)
                c.bump(sy.dma_start(out=c.oT[32:40, 0:128],
                                    in_=c.b2_in.rearrange("(a b) -> a b", b=128)),
                       "dX", 16)
                c.rec("bld", "dX")

            def fvec(v):
                for k in range(NW):
                    c.wtag(v, "wld_%d" % k)
                    c.bump(v.tensor_reduce(out=c.statc(64 + k % 2), in_=c.dgf(k % 4),
                                           axis=AX.X, op=ALU.max,
                                           apply_absolute_value=True), "vV", 1)
                    c.selfsem(v, "vV")
                    dst = c.statc(66) if k < 32 else c.statc(67)
                    if k == 0 or k == 32:
                        c.bump(v.tensor_copy(out=dst, in_=c.statc(64 + k % 2)),
                               "vV", 1)
                    else:
                        c.bump(v.tensor_max(out=dst, in0=dst,
                                            in1=c.statc(64 + k % 2)), "vV", 1)
                    c.rec("wmax_%d" % k, "vV")
                c.selfsem(v, "vV")
                c.rec("wmax_done", "vV")
                c.wtag(v, "xld_all")
                for t in range(NT):
                    c.bump(v.reduce_sum(out=c.statc(32 + t), in_=c.xt(t), axis=AX.X),
                           "vV", 1)
                    c.selfsem(v, "vV")
                    c.rec("xsum_%d" % t, "vV")

            def fscal(s):
                for t in range(NT):
                    c.wtag(s, "xsum_%d" % t)
                    c.bump(s.activation(out=c.statc(t), in_=c.statc(32 + t),
                                        func=AF.Copy, scale=c.slot("c_neg_invD")),
                           "sS", 1)
                    c.selfsem(s, "sS")
                    c.bump(s.activation(out=c.stg1[:], in_=c.xt(t), func=AF.Square,
                                        bias=c.statc(t), scale=1.0,
                                        accum_out=c.statc(32 + t)), "sS", 1)
                    c.selfsem(s, "sS")
                    c.bump(s.activation(out=c.statc(48 + t), in_=c.statc(32 + t),
                                        func=AF.Sqrt, scale=c.slot("c_invD"),
                                        bias=c.slot("c_eps")),
                           "sS", 1)
                    c.selfsem(s, "sS")
                    c.rec("sd_%d" % t, "sS")

            def ftens(tn):
                c.wtag(tn, "bld")
                c.bump(tn.transpose(self.pb[2].ap()[0:128, 0:32],
                                    c.oT[0:32, 0:128], c.identf.ap()[0:32, 0:32]),
                       "tT", 1)
                c.bump(tn.transpose(self.pb[3].ap()[0:128, 0:8],
                                    c.oT[32:40, 0:128], c.identf.ap()[32:40, 32:40]),
                       "tT", 1)
                c.rec("btr", "tT")

            def fvec2(v):
                c.wtag(v, "btr")
                c.bump(v.tensor_copy(out=c.b1c[:], in_=self.pb[2].ap()[0:128, 0:32]),
                       "vV", 1)
                c.bump(v.tensor_copy(out=c.b2c[:], in_=self.pb[3].ap()[0:128, 0:8]),
                       "vV", 1)
                c.selfsem(v, "vV")
                c.bump(v.tensor_scalar_mul(out=c.b2c[:], in0=c.b2c[:], scalar1=0.5),
                       "vV", 1)
                c.rec("bready", "vV")

            def fvec_all(v):
                fvec(v)
                fvec2(v)

            blk.sync(fsync)
            blk.vector(fvec_all)
            blk.scalar(fscal)
            blk.tensor(ftens)

        self.ph_lnfinish_and_ar(ar_idx=0, with_weights=True)

        u = self.uid = self.uid + 1
        assert u == 1
        NWQ = NW
        B3 = self.B.ap().rearrange("p (i t) -> p i t", i=8)
        oTb = self.oT.ap().bitcast(BF16)

        with nc.Block() as blk:
            def fvec(v):
                c.wtag(v, "bcast0")
                self.emit_qscalars(v, c.sc.ap()[:, 0:1], c.sc.ap()[:, 1:2], "q1")
                for (wcol, pre) in ((2, "w1"), (3, "w2")):
                    c.bump(v.tensor_scalar_mul(out=c.slot("sw_" + pre),
                                               in0=c.sc.ap()[:, wcol:wcol + 1],
                                               scalar1=1.0 / 127.0), "vV", 1)
                    c.bump(v.reciprocal(out=c.slot("rws_" + pre),
                                        in_=c.sc.ap()[:, wcol:wcol + 1]), "vV", 1)
                    c.selfsem(v, "vV")
                    c.bump(v.tensor_scalar_mul(out=c.slot("rws_" + pre),
                                               in0=c.slot("rws_" + pre),
                                               scalar1=127.0), "vV", 1)
                    c.selfsem(v, "vV")
                c.bump(v.tensor_tensor(out=c.slot("fac_q1"), in0=c.slot("s_q1"),
                                       in1=c.slot("sw_w1"), op=ALU.mult), "vV", 1)
                c.selfsem(v, "vV")
                c.rec("qs_done_q1", "vV")
                for k in range(32):
                    c.wtag(v, "wq16_%d" % k)
                    if k >= 2:
                        c.wtag(v, "wqs_%d" % (k - 2))
                    c.bump(v.tensor_scalar(out=c.stg1[:], in0=c.stgh[:],
                                           scalar1=ROFF + 127.0,
                                           scalar2=ROFF - 127.0,
                                           op0=ALU.min, op1=ALU.max), "vV", 1)
                    c.bump(v.tensor_scalar(out=c.xqbuf(k % 2), in0=c.stg1[:],
                                           scalar1=ROFF, scalar2=None,
                                           op0=ALU.subtract), "vV", 1)
                    c.rec("wq_%d" % k, "vV")

                def emit_copy(t):
                    c.wtag(v, "xqtr%d_%d" % (u, t))
                    srcp = self.pb[t % 2].ap().bitcast(BF16)
                    srcp = srcp.rearrange("p (i t) -> p i t", i=8)
                    c.bump(v.tensor_copy(out=B3[:, :, t * 128:(t + 1) * 128],
                                         in_=srcp), "vV", 1)
                    c.rec("xqcp%d_%d" % (u, t), "vV")

                for t in range(NT):
                    if t >= 1:
                        emit_copy(t - 1)
                    c.bump(v.tensor_scalar(out=c.stg1[:], in0=c.xt(t),
                                           scalar1=c.statc(t),
                                           scalar2=c.statc(16 + t),
                                           op0=ALU.add, op1=ALU.mult), "vV", 1)
                    c.bump(v.tensor_scalar(out=self.ewh(t % 2, 1024),
                                           in0=c.stg1[:],
                                           scalar1=c.slot("recip_s_q1"),
                                           scalar2=c.slot("zpoff_q1"),
                                           op0=ALU.mult, op1=ALU.add), "vV", 1)
                    c.bump(v.tensor_scalar(out=c.stg1[:],
                                           in0=self.ewh(t % 2, 1024),
                                           scalar1=ROFF + 127.0,
                                           scalar2=ROFF - 128.0,
                                           op0=ALU.min, op1=ALU.max), "vV", 1)
                    c.bump(v.tensor_scalar(out=oTb[:, (t % 2) * 1024:
                                               (t % 2) * 1024 + 1024],
                                           in0=c.stg1[:],
                                           scalar1=c.slot("zpoff_q1"),
                                           scalar2=None, op0=ALU.subtract),
                           "vV", 1)
                    c.rec("xq%d_%d" % (u, t), "vV")
                emit_copy(NT - 1)

                for j in range(32):
                    k = 32 + j
                    c.wtag(v, "wq16_%d" % k)
                    c.wtag(v, "wqs_%d" % (k - 2))
                    c.bump(v.tensor_scalar(out=c.stg1[:], in0=c.stgh[:],
                                           scalar1=ROFF + 127.0,
                                           scalar2=ROFF - 127.0,
                                           op0=ALU.min, op1=ALU.max), "vV", 1)
                    c.bump(v.tensor_scalar(out=c.xqbuf(k % 2), in0=c.stg1[:],
                                           scalar1=ROFF, scalar2=None,
                                           op0=ALU.subtract), "vV", 1)
                    c.rec("wq_%d" % k, "vV")
                    gidx = j
                    ot = gidx % 4
                    c.wtag(v, "gel%d_%d" % (u, gidx))
                    c.bump(v.reduce_max(out=c.statc(64 + gidx),
                                        in_=self.cstgh(ot % 2), axis=AX.X),
                           "vV", 1)
                    c.bump(v.tensor_reduce(out=c.statc(96 + gidx),
                                           in_=self.cstgh(ot % 2), axis=AX.X,
                                           op=ALU.min, negate=True), "vV", 1)
                    c.rec("hst%d_%d" % (u, gidx), "vV")
                for k in range(20):
                    c.wtag(v, "ald_%d" % k)
                    if k >= 2:
                        c.wtag(v, "ast_%d" % (k - 2))
                    ncol = 1024 if k < 12 else 512
                    c.bump(v.tensor_copy(out=c.xqbuf(k % 2)[:, 0:ncol],
                                         in_=c.ewf(k % 2, ncol)), "vV", 1)
                    c.rec("acast_%d" % k, "vV")
                c.selfsem(v, "vV")
                c.bump(v.reduce_max(out=c.statc(62), in_=c.statc(64, 32),
                                    axis=AX.X), "vV", 1)
                c.bump(v.reduce_max(out=c.statc(63), in_=c.statc(96, 32),
                                    axis=AX.X), "vV", 1)
                c.selfsem(v, "vV")
                c.rec("hpack%d" % u, "vV")

            def fsync(sy):
                c.wtag(sy, "qs_done_q1")
                for k in range(NWQ):
                    srcw = c.wsrc(k)
                    if k >= 4:
                        c.wtag(sy, "wq16_%d" % (k - 4))
                    c.bump(sy.dma_start(out=c.cwf(k % 4), in_=srcw),
                           "dW%d" % (k % 4), 16)
                    c.rec("wld2_%d" % k, "dW%d" % (k % 4))
                    if k >= 35 and (k - 35) % 4 == 0 and (k - 35) // 4 < 8:
                        oc = (k - 35) // 4
                        c.wtag(sy, "wqs_%d" % (4 * oc + 3))
                        if oc >= 2:
                            c.wtag(sy, "mm1l%d_%d" % (u, oc - 2))
                        for i in range(8):
                            c.bump(sy.dma_start(
                                out=self.dstg(oc % 2, i * 512, 512),
                                in_=self.wq1s[oc * 512:(oc + 1) * 512,
                                              i * 128:(i + 1) * 128],
                                transpose=True), "dT%d" % (oc % 2), 16)
                        c.rec("w1T%d_%d" % (u, oc), "dT%d" % (oc % 2))
                for k in range(20):
                    if k < 12:
                        w = (c.wq_in, c.wk_in, c.wv_in)[k // 4]
                        srcw = w[(k % 4) * 128:(k % 4 + 1) * 128, :]
                        ncol = 1024
                    else:
                        srcw = c.wo_in[(k - 12) * 128:(k - 11) * 128, :]
                        ncol = 512
                    if k < 2:
                        c.wtag(sy, "wq16_%d" % (NWQ - 2 + k))
                    else:
                        c.wtag(sy, "acast_%d" % (k - 2))
                    c.bump(sy.dma_start(out=c.ewf(k % 2, ncol), in_=srcw),
                           "dW%d" % (k % 2), 16)
                    c.rec("ald_%d" % k, "dW%d" % (k % 2))

            def fscal(s):
                c.wtag(s, "qs_done_q1")
                c.wtag(s, "bready")
                for k in range(32):
                    c.wtag(s, "wld2_%d" % k)
                    if k >= 1:
                        c.wtag(s, "wq_%d" % (k - 1))
                    c.bump(s.activation(out=c.stgh[:], in_=c.cwf(k % 4),
                                        func=AF.Identity, bias=c.slot("c_roff"),
                                        scale=c.slot("rws_w1")), "sS", 1)
                    c.rec("wq16_%d" % k, "sS")
                for j in range(32):
                    k = 32 + j
                    c.wtag(s, "wld2_%d" % k)
                    c.wtag(s, "wq_%d" % (k - 1))
                    c.bump(s.activation(out=c.stgh[:], in_=c.cwf(k % 4),
                                        func=AF.Identity, bias=c.slot("c_roff"),
                                        scale=c.slot("rws_w2")), "sS", 1)
                    c.rec("wq16_%d" % k, "sS")
                    gidx = j
                    ot = gidx % 4
                    c.wtag(s, "mm1%d_%d" % (u, gidx))
                    if gidx >= 2:
                        c.wtag(s, "hsp%d_%d" % (u, gidx - 2))
                        c.wtag(s, "hst%d_%d" % (u, gidx - 2))
                    for tq in range(4):
                        bank = (ot % 2) * 4 + tq
                        inst = s.activation(
                            out=self.cstgh(ot % 2)[:, tq * 512:(tq + 1) * 512],
                            in_=self.pb[bank].ap(), func=AF.Gelu,
                            scale=c.slot("fac_q1"),
                            bias=c.b1c.ap()[:, gidx:gidx + 1])
                        if tq == 3:
                            c.bump(inst, "sS", 1)
                    c.rec("gel%d_%d" % (u, gidx), "sS")

            def fgps(g):
                for k in range(32):
                    c.wtag(g, "wq_%d" % k)
                    c.bump(g.dma_start(out=c.wdst(k), in_=c.xqbuf(k % 2)),
                           "dG", 16)
                    c.rec("wqs_%d" % k, "dG")
                for j in range(32):
                    k = 32 + j
                    c.wtag(g, "wq_%d" % k)
                    c.bump(g.dma_start(out=c.wdst(k), in_=c.xqbuf(k % 2)),
                           "dG", 16)
                    c.rec("wqs_%d" % k, "dG")
                    gidx = j
                    oc, ot = gidx // 4, gidx % 4
                    c.wtag(g, "gel%d_%d" % (u, gidx))
                    c.bump(g.dma_start(
                        out=self.h_sp[oc * 512 + ot * 128:
                                      oc * 512 + (ot + 1) * 128, :],
                        in_=self.cstgh(ot % 2)), "dP%d" % (ot % 2), 16)
                    c.rec("hsp%d_%d" % (u, gidx), "dP%d" % (ot % 2))
                for k in range(20):
                    c.wtag(g, "acast_%d" % k)
                    if k < 12:
                        wd = (c.wabq, c.wabk, c.wabv)[k // 4]
                        dst = wd[(k % 4) * 128:(k % 4 + 1) * 128, :]
                        ncol = 1024
                    else:
                        dst = c.wabo[(k - 12) * 128:(k - 11) * 128, :]
                        ncol = 512
                    c.bump(g.dma_start(out=dst, in_=c.xqbuf(k % 2)[:, 0:ncol]),
                           "dG", 16)
                    c.rec("ast_%d" % k, "dG")
                c.rec("wprep_done", "dG")

            def ftens(tn):
                for t in range(NT):
                    c.wtag(tn, "xq%d_%d" % (u, t))
                    if t >= 2:
                        c.wtag(tn, "xqcp%d_%d" % (u, t - 2))
                    xqb = oTb[:, (t % 2) * 1024:(t % 2) * 1024 + 1024]
                    dstp = self.pb[t % 2].ap().bitcast(BF16)
                    for i in range(8):
                        inst = tn.transpose(dstp[:, i * 128:(i + 1) * 128],
                                            xqb[:, i * 128:(i + 1) * 128],
                                            self.identb.ap())
                        if i == 7:
                            c.bump(inst, "tT", 1)
                    c.rec("xqtr%d_%d" % (u, t), "tT")
                c.wtag(tn, "xqcp%d_%d" % (u, NT - 1))
                for oc in range(8):
                    c.wtag(tn, "w1T%d_%d" % (u, oc))
                    for ot in range(4):
                        gidx = oc * 4 + ot
                        if gidx >= 2:
                            c.wtag(tn, "gel%d_%d" % (u, gidx - 2))
                        for i in range(8):
                            lhsT = self.dstg(oc % 2, i * 512 + ot * 128, 128)
                            for tq in range(4):
                                bank = (ot % 2) * 4 + tq
                                rhs = self.B.ap()[:, i * 2048 + tq * 512:
                                                  i * 2048 + (tq + 1) * 512]
                                inst = tn.matmul(self.pb[bank].ap(), lhsT, rhs,
                                                 start=(i == 0), stop=(i == 7))
                                if i == 7 and tq == 3:
                                    c.bump(inst, "tT", 1)
                        c.rec("mm1%d_%d" % (u, gidx), "tT")
                    c.rec("mm1l%d_%d" % (u, oc), "tT")

            blk.vector(fvec)
            blk.sync(fsync)
            blk.scalar(fscal)
            blk.gpsimd(fgps)
            blk.tensor(ftens)

    def ph_lnfinish_and_ar(self, ar_idx, with_weights):
        c = self
        nc = self.nc
        sfx = "_%d" % ar_idx

        with nc.Block() as blk:
            def fvec(v):
                for t in range(NT):
                    c.wtag(v, "sd_%d" % t if ar_idx == 0 else "sd3_%d" % t)
                    c.bump(v.reciprocal(out=c.statc(16 + t), in_=c.statc(48 + t)),
                           "vV", 1)
                c.selfsem(v, "vV")
                for t in range(NT):
                    c.bump(v.tensor_scalar(out=c.stg1[:], in0=c.xt(t),
                                           scalar1=c.statc(t), scalar2=c.statc(16 + t),
                                           op0=ALU.add, op1=ALU.mult), "vV", 1)
                    c.bump(v.reduce_max(out=c.statc(68 + t), in_=c.stg1[:],
                                        axis=AX.X), "vV", 1)
                    c.bump(v.tensor_reduce(out=c.statc(84 + t), in_=c.stg1[:],
                                           axis=AX.X, op=ALU.min, negate=True),
                           "vV", 1)
                c.selfsem(v, "vV")
                c.bump(v.reduce_max(out=c.statc(104), in_=c.statc(68, 16), axis=AX.X),
                       "vV", 1)
                c.bump(v.reduce_max(out=c.statc(105), in_=c.statc(84, 16), axis=AX.X),
                       "vV", 1)
                if with_weights:
                    c.wtag(v, "wmax_done")
                    c.bump(v.tensor_copy(out=c.statc(106), in_=c.statc(66)), "vV", 1)
                    c.bump(v.tensor_copy(out=c.statc(107), in_=c.statc(67)), "vV", 1)
                c.selfsem(v, "vV")
                c.rec("pack" + sfx, "vV")
                c.wtag(v, "stat_tr" + sfx)
                c.bump(v.tensor_copy(out=c.stg1[0:4, 0:128],
                                     in_=self.pb[0].ap()[0:4, 0:128]), "vV", 1)
                c.selfsem(v, "vV")
                c.bump(v.reduce_max(out=c.stg1[0:4, 128:129],
                                    in_=c.stg1[0:4, 0:128], axis=AX.X), "vV", 1)
                c.selfsem(v, "vV")
                c.rec("glob4" + sfx, "vV")

            def ftens(tn):
                c.wtag(tn, "pack" + sfx)
                c.bump(tn.transpose(self.pb[0].ap()[0:4, 0:128],
                                    c.statc(104, 4), c.identf.ap()), "tT", 1)
                c.rec("stat_tr" + sfx, "tT")

            def fgps(g):
                c.wtag(g, "glob4" + sfx)
                c.bump(g.dma_start(out=c.ar_i[ar_idx].ap(),
                                   in_=c.stg1[0:2, 128:129]), "dG", 16)
                if with_weights:
                    c.bump(g.dma_start(out=c.scw.ap()[0:2],
                                       in_=c.stg1[2:4, 128:129]), "dG", 16)
                c.wall(g, "dG")
                c.bump(g.collective_compute(
                    "AllReduce", ALU.max, replica_groups=[list(range(NC))],
                    ins=[c.ar_i[ar_idx].ap().opt()],
                    outs=[c.ar_o[ar_idx].ap().opt()]), "cC", 1)
                c.wall(g, "cC")
                c.bump(g.dma_start(
                    out=c.sc.ap()[:, 0:2],
                    in_=c.ar_o[ar_idx].ap().unsqueeze(0).broadcast_to([128, 2])),
                    "dG", 16)
                if with_weights:
                    c.bump(g.dma_start(
                        out=c.sc.ap()[:, 2:4],
                        in_=c.scw.ap()[0:2].unsqueeze(0).broadcast_to([128, 2])),
                        "dG", 16)
                c.rec("bcast%d" % ar_idx, "dG")

            blk.vector(fvec)
            blk.tensor(ftens)
            blk.gpsimd(fgps)

    def emit_qscalars(self, v, amax, negmin, pre):
        c = self
        c.bump(v.tensor_tensor(out=c.slot("d_" + pre), in0=amax, in1=negmin,
                               op=ALU.add), "vV", 1)
        c.selfsem(v, "vV")
        c.bump(v.tensor_scalar(out=c.slot("s_" + pre), in0=c.slot("d_" + pre),
                               scalar1=1.0 / 255.0, scalar2=1e-8,
                               op0=ALU.mult, op1=ALU.max), "vV", 1)
        c.selfsem(v, "vV")
        c.bump(v.reciprocal(out=c.slot("recip_s_" + pre), in_=c.slot("s_" + pre)),
               "vV", 1)
        c.selfsem(v, "vV")
        c.bump(v.tensor_tensor(out=c.slot("t_" + pre), in0=negmin,
                               in1=c.slot("recip_s_" + pre), op=ALU.mult), "vV", 1)
        c.selfsem(v, "vV")
        c.bump(v.tensor_scalar(out=c.stgh[:, 0:1], in0=c.slot("t_" + pre),
                               scalar1=1024.0, scalar2=None, op0=ALU.add), "vV", 1)
        c.selfsem(v, "vV")
        c.bump(v.tensor_scalar(out=c.slot("zpoff_" + pre), in0=c.stgh[:, 0:1],
                               scalar1=384.0, scalar2=ROFF + 127.0,
                               op0=ALU.add, op1=ALU.min), "vV", 1)
        c.selfsem(v, "vV")

    def ph_ffn(self, ar_idx, qpre_in, qpre_h):
        c = self
        nc = self.nc
        if ar_idx == 2:
            u = self.uid = self.uid + 1
        else:
            u = self.uid

        if ar_idx == 2:
            with nc.Block() as blk:
                def fvec(v):
                    for t in range(NT):
                        c.bump(v.reduce_sum(out=c.statc(32 + t), in_=c.xt(t),
                                            axis=AX.X), "vV", 1)
                        c.selfsem(v, "vV")
                        c.rec("xsum3_%d" % t, "vV")

                def fscal(s):
                    for t in range(NT):
                        c.wtag(s, "xsum3_%d" % t)
                        c.bump(s.activation(out=c.statc(t), in_=c.statc(32 + t),
                                            func=AF.Copy,
                                            scale=c.slot("c_neg_invD")), "sS", 1)
                        c.selfsem(s, "sS")
                        c.bump(s.activation(out=c.stg1[:], in_=c.xt(t),
                                            func=AF.Square, bias=c.statc(t),
                                            scale=1.0, accum_out=c.statc(32 + t)),
                               "sS", 1)
                        c.selfsem(s, "sS")
                        c.bump(s.activation(out=c.statc(48 + t), in_=c.statc(32 + t),
                                            func=AF.Sqrt, scale=c.slot("c_invD"),
                                        bias=c.slot("c_eps")),
                               "sS", 1)
                        c.selfsem(s, "sS")
                        c.rec("sd3_%d" % t, "sS")

                blk.vector(fvec)
                blk.scalar(fscal)
            self.ph_lnfinish_and_ar(ar_idx=ar_idx, with_weights=False)
            with nc.Block() as blk:
                def fvec(v):
                    c.wtag(v, "bcast%d" % ar_idx)
                    self.emit_qscalars(v, c.sc.ap()[:, 0:1], c.sc.ap()[:, 1:2],
                                       qpre_in)
                    c.bump(v.tensor_tensor(out=c.slot("fac_" + qpre_in),
                                           in0=c.slot("s_" + qpre_in),
                                           in1=c.slot("sw_w1"), op=ALU.mult), "vV", 1)
                    c.selfsem(v, "vV")
                    c.rec("qs_done_%s" % qpre_in, "vV")
                blk.vector(fvec)

        if ar_idx == 2:
            B3 = self.B.ap().rearrange("p (i t) -> p i t", i=8)
            with nc.Block() as blk:
                def fvec(v):
                    c.wtag(v, "qs_done_%s" % qpre_in)

                    def emit_copy(t):
                        c.wtag(v, "xqtr%d_%d" % (u, t))
                        src = self.pb[t % 2].ap().bitcast(BF16)
                        src = src.rearrange("p (i t) -> p i t", i=8)
                        c.bump(v.tensor_copy(out=B3[:, :, t * 128:(t + 1) * 128],
                                             in_=src), "vV", 1)
                        c.rec("xqcp%d_%d" % (u, t), "vV")

                    for t in range(NT):
                        if t >= 1:
                            emit_copy(t - 1)
                        xqb = c.xqbuf(t % 2)
                        c.bump(v.tensor_scalar(out=c.stg1[:], in0=c.xt(t),
                                               scalar1=c.statc(t), scalar2=c.statc(16 + t),
                                               op0=ALU.add, op1=ALU.mult), "vV", 1)
                        c.bump(v.tensor_scalar(out=c.stgh[:], in0=c.stg1[:],
                                               scalar1=c.slot("recip_s_" + qpre_in),
                                               scalar2=c.slot("zpoff_" + qpre_in),
                                               op0=ALU.mult, op1=ALU.add), "vV", 1)
                        c.bump(v.tensor_scalar(out=c.stg1[:], in0=c.stgh[:],
                                               scalar1=ROFF + 127.0, scalar2=ROFF - 128.0,
                                               op0=ALU.min, op1=ALU.max), "vV", 1)
                        c.bump(v.tensor_scalar(out=xqb, in0=c.stg1[:],
                                               scalar1=c.slot("zpoff_" + qpre_in),
                                               scalar2=None, op0=ALU.subtract), "vV", 1)
                        c.rec("xq%d_%d" % (u, t), "vV")
                    emit_copy(NT - 1)

                def ftens(tn):
                    for t in range(NT):
                        c.wtag(tn, "xq%d_%d" % (u, t))
                        if t >= 2:
                            c.wtag(tn, "xqcp%d_%d" % (u, t - 2))
                        xqb = c.xqbuf(t % 2)
                        dstp = self.pb[t % 2].ap().bitcast(BF16)
                        for i in range(8):
                            inst = tn.transpose(dstp[:, i * 128:(i + 1) * 128],
                                                xqb[:, i * 128:(i + 1) * 128],
                                                self.identb.ap())
                            if i == 7:
                                c.bump(inst, "tT", 1)
                        c.rec("xqtr%d_%d" % (u, t), "tT")

                blk.vector(fvec)
                blk.tensor(ftens)

            if u == 1:
                self.dump("d_xqT", src_ap=self.B.ap())
            with nc.Block() as blk:
                def fsync(sy):
                    c.wtag(sy, "wqs_31")
                    for oc in range(8):
                        if oc >= 2:
                            c.wtag(sy, "mm1l%d_%d" % (u, oc - 2))
                        for i in range(8):
                            c.bump(sy.dma_start(
                                out=self.dstg(oc % 2, i * 512, 512),
                                in_=self.wq1s[oc * 512:(oc + 1) * 512,
                                              i * 128:(i + 1) * 128],
                                transpose=True), "dT%d" % (oc % 2), 16)
                        c.rec("w1T%d_%d" % (u, oc), "dT%d" % (oc % 2))

                def ftens(tn):
                    for oc in range(8):
                        c.wtag(tn, "w1T%d_%d" % (u, oc))
                        for ot in range(4):
                            gidx = oc * 4 + ot
                            if gidx >= 2:
                                c.wtag(tn, "gel%d_%d" % (u, gidx - 2))
                            for i in range(8):
                                lhsT = self.dstg(oc % 2, i * 512 + ot * 128, 128)
                                for tq in range(4):
                                    bank = (ot % 2) * 4 + tq
                                    rhs = self.B.ap()[:, i * 2048 + tq * 512:
                                                      i * 2048 + (tq + 1) * 512]
                                    inst = tn.matmul(self.pb[bank].ap(), lhsT, rhs,
                                                     start=(i == 0), stop=(i == 7))
                                    if i == 7 and tq == 3:
                                        c.bump(inst, "tT", 1)
                            c.rec("mm1%d_%d" % (u, gidx), "tT")
                        c.rec("mm1l%d_%d" % (u, oc), "tT")

                def fscal(s):
                    c.wtag(s, "bready")
                    for gidx in range(32):
                        ot = gidx % 4
                        c.wtag(s, "mm1%d_%d" % (u, gidx))
                        if gidx >= 2:
                            c.wtag(s, "hsp%d_%d" % (u, gidx - 2))
                            c.wtag(s, "hst%d_%d" % (u, gidx - 2))
                        for tq in range(4):
                            bank = (ot % 2) * 4 + tq
                            inst = s.activation(
                                out=self.cstgh(ot % 2)[:, tq * 512:(tq + 1) * 512],
                                in_=self.pb[bank].ap(), func=AF.Gelu,
                                scale=c.slot("fac_" + qpre_in),
                                bias=c.b1c.ap()[:, gidx:gidx + 1])
                            if tq == 3:
                                c.bump(inst, "sS", 1)
                        c.rec("gel%d_%d" % (u, gidx), "sS")

                def fvec(v):
                    for gidx in range(32):
                        ot = gidx % 4
                        c.wtag(v, "gel%d_%d" % (u, gidx))
                        c.bump(v.reduce_max(out=c.statc(64 + gidx),
                                            in_=self.cstgh(ot % 2), axis=AX.X), "vV", 1)
                        c.bump(v.tensor_reduce(out=c.statc(96 + gidx),
                                               in_=self.cstgh(ot % 2), axis=AX.X,
                                               op=ALU.min, negate=True), "vV", 1)
                        c.rec("hst%d_%d" % (u, gidx), "vV")
                    c.selfsem(v, "vV")
                    c.bump(v.reduce_max(out=c.statc(62), in_=c.statc(64, 32),
                                        axis=AX.X), "vV", 1)
                    c.bump(v.reduce_max(out=c.statc(63), in_=c.statc(96, 32),
                                        axis=AX.X), "vV", 1)
                    c.selfsem(v, "vV")
                    c.rec("hpack%d" % u, "vV")

                def fgps(g):
                    for gidx in range(32):
                        oc, ot = gidx // 4, gidx % 4
                        c.wtag(g, "gel%d_%d" % (u, gidx))
                        c.bump(g.dma_start(
                            out=self.h_sp[oc * 512 + ot * 128: oc * 512 + (ot + 1) * 128,
                                          :],
                            in_=self.cstgh(ot % 2)), "dP%d" % (ot % 2), 16)
                        c.rec("hsp%d_%d" % (u, gidx), "dP%d" % (ot % 2))

                blk.sync(fsync)
                blk.tensor(ftens)
                blk.scalar(fscal)
                blk.vector(fvec)
                blk.gpsimd(fgps)

        hq_ar = ar_idx + 1

        def _w2ld(eng, o, dw):
            c.bump(eng.dma_start(
                out=self.C.ap()[:, o * 1024:(o + 1) * 1024],
                in_=self.wq2s[:, o * 128:(o + 1) * 128],
                transpose=True), dw, 16)

        with nc.Block() as blk:
            def fsync(sy):
                c.wall(sy, "dG")
                for o in range(4, 20):
                    _w2ld(sy, o, "dW0")
                c.rec("w2Ta%d" % u, "dW0")
                for o in range(20, 32):
                    _w2ld(sy, o, "dW1")
                c.rec("w2Tb%d" % u, "dW1")
                c.wtag(sy, "hsp%d_30" % u)
                c.wtag(sy, "hsp%d_31" % u)
                for o in range(4):
                    _w2ld(sy, o, "dW3")
                c.rec("w2Td%d" % u, "dW3")

            def ftens(tn):
                c.wtag(tn, "hpack%d" % u)
                c.bump(tn.transpose(self.pb[0].ap()[0:2, 0:128],
                                    c.statc(62, 2), self.identf.ap()), "tT", 1)
                c.rec("htr%d" % u, "tT")

            def fvec(v):
                c.wtag(v, "htr%d" % u)
                c.bump(v.tensor_copy(out=c.stg1[0:2, 0:128],
                                     in_=self.pb[0].ap()[0:2, 0:128]), "vV", 1)
                c.selfsem(v, "vV")
                c.bump(v.reduce_max(out=c.stg1[0:2, 128:129],
                                    in_=c.stg1[0:2, 0:128], axis=AX.X), "vV", 1)
                c.selfsem(v, "vV")
                c.rec("hglob%d" % u, "vV")
                c.wtag(v, "b2l%d" % u)
                c.bump(v.tensor_scalar_mul(out=self.b2v(), in0=self.b2v(),
                                           scalar1=0.5), "vV", 1)
                for t in range(NT):
                    v.tensor_tensor(out=c.xt(t), in0=c.xt(t),
                                    in1=self.b2v(), op=ALU.add)
                c.wtag(v, "hbc%d" % u)
                self.emit_qscalars(v, c.sc.ap()[:, 0:1], c.sc.ap()[:, 1:2], qpre_h)
                c.bump(v.tensor_tensor(out=c.slot("fac_" + qpre_h),
                                       in0=c.slot("s_" + qpre_h),
                                       in1=c.slot("sw_w2"), op=ALU.mult), "vV", 1)
                c.selfsem(v, "vV")
                c.bump(v.tensor_scalar_mul(out=c.slot("fac_" + qpre_h),
                                           in0=c.slot("fac_" + qpre_h),
                                           scalar1=0.5), "vV", 1)
                c.selfsem(v, "vV")
                c.rec("qs_done_%s" % qpre_h, "vV")

            def fgps(g):
                c.bump(g.dma_start(
                    out=self.b2v(),
                    in_=self.b2_in.unsqueeze(0).broadcast_to([128, 1024])),
                    "dB2", 16)
                c.rec("b2l%d" % u, "dB2")
                c.wtag(g, "hglob%d" % u)
                c.bump(g.dma_start(out=c.ar_i[hq_ar].ap(),
                                   in_=c.stg1[0:2, 128:129]), "dG", 16)
                c.wall(g, "dG")
                c.bump(g.collective_compute(
                    "AllReduce", ALU.max, replica_groups=[list(range(NC))],
                    ins=[c.ar_i[hq_ar].ap().opt()],
                    outs=[c.ar_o[hq_ar].ap().opt()]), "cC", 1)
                c.wall(g, "cC")
                c.bump(g.dma_start(
                    out=c.sc.ap()[:, 0:2],
                    in_=c.ar_o[hq_ar].ap().unsqueeze(0).broadcast_to([128, 2])),
                    "dG", 16)
                c.rec("hbc%d" % u, "dG")

            blk.sync(fsync)
            blk.tensor(ftens)
            blk.vector(fvec)
            blk.gpsimd(fgps)

        with nc.Block() as blk:
            def fsync(sy):
                for tq in range(4):
                    for o in range(32):
                        k = tq * 32 + o
                        par, j = (k // 8) % 2, k % 8
                        if k >= 16:
                            c.wtag(sy, "hq%d_%d" % (u, k - 16))
                        c.wtag(sy, "hsp%d_%d" % (u, o))
                        c.bump(sy.dma_start(
                            out=self.dstgh(par, j * 512, 512),
                            in_=self.h_sp[o * 128:(o + 1) * 128,
                                          tq * 512:(tq + 1) * 512]),
                            "dH%d" % par, 16)
                        if j == 7:
                            c.rec("hldg%d_%d" % (u, k // 8), "dH%d" % par)

            def fscal(s):
                c.wtag(s, "qs_done_%s" % qpre_h)
                for tq in range(4):
                    for o in range(32):
                        k = tq * 32 + o
                        if k % 8 == 0:
                            c.wtag(s, "hldg%d_%d" % (u, k // 8))
                        if k >= 8:
                            c.wtag(s, "hq%d_%d" % (u, k - 8))
                        par, j = (k // 8) % 2, k % 8
                        c.bump(s.activation(
                            out=self.ewh(0, 4096)[:, (k % 8) * 512:
                                                  (k % 8) * 512 + 512],
                            in_=self.dstgh(par, j * 512, 512), func=AF.Identity,
                            scale=c.slot("recip_s_" + qpre_h),
                            bias=c.slot("zpoff_" + qpre_h)), "sS", 1)
                        c.rec("hq16%d_%d" % (u, k), "sS")

            def _radd(v, t):
                c.wtag(v, "mm2t%d_%d" % (u, t))
                pbt = self.pbt01 if t % 2 == 0 else self.pbt23
                c.bump(v.tensor_scalar(
                    out=self.stg1.ap(), in0=pbt.ap(),
                    scalar1=c.slot("fac_" + qpre_h), scalar2=None,
                    op0=ALU.mult), "vV", 1)
                c.bump(v.tensor_tensor(out=c.xt(t), in0=c.xt(t),
                                       in1=self.stg1.ap(), op=ALU.add), "vV", 1)
                c.rec("radd%d_%d" % (u, t), "vV")

            def fvec(v):
                for tq in range(4):
                    if tq >= 1:
                        _radd(v, (tq - 1) * 4 + 0)
                        _radd(v, (tq - 1) * 4 + 1)
                    for o in range(32):
                        k = tq * 32 + o
                        if tq >= 1 and o == 12:
                            _radd(v, (tq - 1) * 4 + 2)
                        if tq >= 1 and o == 24:
                            _radd(v, (tq - 1) * 4 + 3)
                        c.wtag(v, "hq16%d_%d" % (u, k))
                        if tq >= 1:
                            c.wtag(v, "mmB%d_%d_%d" % (u, tq - 1, o))
                        c.bump(v.tensor_scalar(
                            out=self.stg1.ap()[:, 0:512],
                            in0=self.ewh(0, 4096)[:, (k % 8) * 512:
                                                  (k % 8) * 512 + 512],
                            scalar1=ROFF + 127.0, scalar2=ROFF - 128.0,
                            op0=ALU.min, op1=ALU.max), "vV", 1)
                        c.bump(v.tensor_scalar(
                            out=self.B.ap()[:, o * 512:(o + 1) * 512],
                            in0=self.stg1.ap()[:, 0:512],
                            scalar1=c.slot("zpoff_" + qpre_h), scalar2=None,
                            op0=ALU.subtract), "vV", 1)
                        c.rec("hq%d_%d" % (u, k), "vV")
                for t in (12, 13, 14, 15):
                    _radd(v, t)

            def ftens(tn):
                o_order = list(range(4, 32)) + [0, 1, 2, 3]
                for tile in range(16):
                    tq, tt = tile // 4, tile % 4
                    if tile >= 2:
                        c.wtag(tn, "radd%d_%d" % (u, tile - 2))
                    for oi, o in enumerate(o_order):
                        if tile == 0:
                            if o == 4:
                                c.wtag(tn, "w2Ta%d" % u)
                            elif o == 20:
                                c.wtag(tn, "w2Tb%d" % u)
                            elif o == 0:
                                c.wtag(tn, "w2Td%d" % u)
                        if tt == 0:
                            c.wtag(tn, "hq%d_%d" % (u, tq * 32 + o))
                        for fh in range(2):
                            inst = tn.matmul(
                                self.pb[2 * (tile % 2) + fh].ap(),
                                self.B.ap()[:, o * 512 + tt * 128:
                                            o * 512 + tt * 128 + 128],
                                self.C.ap()[:, o * 1024 + fh * 512:
                                            o * 1024 + fh * 512 + 512],
                                start=(oi == 0), stop=(oi == 31))
                            if tt == 3 and fh == 1:
                                c.bump(inst, "tT", 1)
                                c.rec("mmB%d_%d_%d" % (u, tq, o), "tT")
                    if tt != 3:
                        c.bump(inst, "tT", 1)
                    c.rec("mm2t%d_%d" % (u, tile), "tT")

            blk.sync(fsync)
            blk.scalar(fscal)
            blk.vector(fvec)
            blk.tensor(ftens)

    def _emit_oev(self, s, u, gidx, qpre_h):
        c = self
        dt = gidx % 8
        c.wtag(s, "mm2%d_%d" % (u, gidx))
        if gidx >= 2:
            c.wtag(s, "otr%d_%d" % (u, gidx - 2))
        c.bump(s.activation(
            out=self.oT.ap()[:, (gidx % 2) * 512:(gidx % 2) * 512 + 512],
            in_=self.pb[(0, 1, 4, 5)[gidx % 4]].ap(), func=AF.Identity,
            scale=c.slot("fac_" + qpre_h),
            bias=self.b2c.ap()[:, dt:dt + 1]), "sS", 1)
        c.rec("oev%d_%d" % (u, gidx), "sS")

    def _emit_otr(self, tn, u, gidx):
        c = self
        c.wtag(tn, "oev%d_%d" % (u, gidx))
        if gidx >= 2:
            c.wtag(tn, "radd%d_%d" % (u, gidx - 2))
        src = self.oT.ap()[:, (gidx % 2) * 512:(gidx % 2) * 512 + 512]
        dstp = self.pb[6 + gidx % 2].ap()
        for j in range(4):
            inst = tn.transpose(dstp[:, j * 128:(j + 1) * 128],
                                src[:, j * 128:(j + 1) * 128], self.identf.ap())
            if j == 3:
                c.bump(inst, "tT", 1)
        c.rec("otr%d_%d" % (u, gidx), "tT")

    def _emit_radd(self, v, u, gidx):
        c = self
        tq, dt = gidx // 8, gidx % 8
        c.wtag(v, "otr%d_%d" % (u, gidx))
        pbt = self.pb[6 + gidx % 2].ap()
        for j in range(4):
            tdst = c.xt(tq * 4 + j)[:, dt * 128:(dt + 1) * 128]
            inst = v.tensor_tensor(out=tdst, in0=tdst,
                                   in1=pbt[:, j * 128:(j + 1) * 128], op=ALU.add)
            if j == 3:
                c.bump(inst, "vV", 1)
        c.rec("radd%d_%d" % (u, gidx), "vV")

    def ph_attn(self):
        c = self
        nc = self.nc
        B3 = self.B.ap().rearrange("p (i t) -> p i t", i=8)

        with nc.Block() as blk:
            def fvec(v):
                for t in range(NT):
                    c.bump(v.reduce_sum(out=c.statc(32 + t), in_=c.xt(t),
                                        axis=AX.X), "vV", 1)
                    c.selfsem(v, "vV")
                    c.rec("asum_%d" % t, "vV")
                for t in range(NT):
                    c.wtag(v, "asd_%d" % t)
                    c.bump(v.reciprocal(out=c.statc(16 + t), in_=c.statc(48 + t)),
                           "vV", 1)
                    c.selfsem(v, "vV")
                    xqb = c.xqbuf(t % 2)
                    if t >= 2:
                        c.wtag(v, "atr_%d" % (t - 2))
                    c.bump(v.tensor_scalar(out=xqb, in0=c.xt(t),
                                           scalar1=c.statc(t), scalar2=c.statc(16 + t),
                                           op0=ALU.add, op1=ALU.mult), "vV", 1)
                    c.rec("axn_%d" % t, "vV")
                    if t >= 1:
                        c.wtag(v, "atr_%d" % (t - 1))
                        src = self.pb[(t - 1) % 2].ap().bitcast(BF16)
                        src = src.rearrange("p (i t) -> p i t", i=8)
                        c.bump(v.tensor_copy(out=B3[:, :, (t - 1) * 128:t * 128],
                                             in_=src), "vV", 1)
                        c.rec("acp_%d" % (t - 1), "vV")
                c.wtag(v, "atr_%d" % (NT - 1))
                src = self.pb[(NT - 1) % 2].ap().bitcast(BF16)
                src = src.rearrange("p (i t) -> p i t", i=8)
                c.bump(v.tensor_copy(out=B3[:, :, (NT - 1) * 128:NT * 128], in_=src),
                       "vV", 1)
                c.rec("acp_%d" % (NT - 1), "vV")

            def fscal(s):
                for t in range(NT):
                    c.wtag(s, "asum_%d" % t)
                    c.bump(s.activation(out=c.statc(t), in_=c.statc(32 + t),
                                        func=AF.Copy, scale=c.slot("c_neg_invD")),
                           "sS", 1)
                    c.selfsem(s, "sS")
                    c.bump(s.activation(out=c.stg1[:], in_=c.xt(t), func=AF.Square,
                                        bias=c.statc(t), scale=1.0,
                                        accum_out=c.statc(32 + t)), "sS", 1)
                    c.selfsem(s, "sS")
                    c.bump(s.activation(out=c.statc(48 + t), in_=c.statc(32 + t),
                                        func=AF.Sqrt, scale=c.slot("c_invD"),
                                        bias=c.slot("c_eps")),
                           "sS", 1)
                    c.selfsem(s, "sS")
                    c.rec("asd_%d" % t, "sS")

            def ftens(tn):
                for t in range(NT):
                    c.wtag(tn, "axn_%d" % t)
                    if t >= 2:
                        c.wtag(tn, "acp_%d" % (t - 2))
                    xqb = c.xqbuf(t % 2)
                    dstp = self.pb[t % 2].ap().bitcast(BF16)
                    for i in range(8):
                        inst = tn.transpose(dstp[:, i * 128:(i + 1) * 128],
                                            xqb[:, i * 128:(i + 1) * 128],
                                            self.identb.ap())
                        if i == 7:
                            c.bump(inst, "tT", 1)
                    c.rec("atr_%d" % t, "tT")

            def fgps(g):
                for t in range(NT):
                    c.wtag(g, "axn_%d" % t)
                    c.bump(g.dma_start(out=self.x_sp[t * 128:(t + 1) * 128, :],
                                       in_=c.xt(t)), "dG", 16)
                c.rec("xspill", "dG")
                c.wtag(g, "acp_%d" % (NT - 1))
                for it in range(8):
                    c.bump(g.dma_start(
                        out=self.ag_i.ap()[it * 128:(it + 1) * 128, :],
                        in_=self.B.ap()[:, it * 2048:(it + 1) * 2048]), "dG", 16)
                c.wall(g, "dG")
                c.bump(g.collective_compute(
                    "AllGather", ALU.bypass,
                    replica_groups=[[0, 1], [2, 3], [4, 5], [6, 7]],
                    ins=[self.ag_i.ap().opt()], outs=[self.ag_o.ap().opt()]),
                    "cC", 1)
                c.wall(g, "cC")
                for j in range(2):
                    for it in range(8):
                        c.bump(g.dma_start(
                            out=self.C.ap()[:, it * 4096 + j * 2048:
                                            it * 4096 + (j + 1) * 2048],
                            in_=self.ag_o.ap()[j, it * 128:(it + 1) * 128, :]),
                            "dG", 16)
                c.rec("xnT_done", "dG")

            blk.vector(fvec)
            blk.scalar(fscal)
            blk.tensor(ftens)
            blk.gpsimd(fgps)

        with nc.Block() as blk:
            def fsync(sy):
                c.wtag(sy, "wprep_done")
                for it in range(8):
                    c.bump(sy.dma_start(
                        out=self.Ew.ap()[:, it * 512:(it + 1) * 512],
                        in_=self.wabv[0:512, it * 128:(it + 1) * 128],
                        transpose=True), "dW0", 16)
                c.rec("wvT", "dW0")
                for it in range(8):
                    c.bump(sy.dma_start(
                        out=self.dstg(0, it * 512, 512),
                        in_=self.wabk[0:512, it * 128:(it + 1) * 128],
                        transpose=True), "dW1", 16)
                c.rec("wkT", "dW1")
                c.wtag(sy, "vev_31")
                for it in range(8):
                    c.bump(sy.dma_start(
                        out=self.Ew.ap()[:, it * 512:(it + 1) * 512],
                        in_=self.wabq[0:512, it * 128:(it + 1) * 128],
                        transpose=True), "dW0", 16)
                c.rec("wqT", "dW0")

            def fgps(g):
                a30 = self.A.ap().bitcast(BF16)[:, VBASE:VBASE + 31 * 520]
                a30 = a30.rearrange("p (tt h c) -> p tt h c", h=NH, c=65)
                c.bump(g.memset(a30[:, :, :, 64:65], 1.0), "gG", 1)
                cx = self.ctx_x.ap()[:, 0:520].rearrange("p (h c) -> p h c", c=65)
                c.bump(g.memset(cx[:, :, 64:65], 1.0), "gG", 1)
                c.rec("vones", "gG")

            def ftens(tn):
                c.wtag(tn, "xnT_done")
                c.wtag(tn, "wvT")
                for tt in range(32):
                    if tt >= 2:
                        c.wtag(tn, "vev_%d" % (tt - 2))
                    for i in range(8):
                        lhsT = self.xnT(i, tt * 128, 128)
                        inst = tn.matmul(self.pb[tt % 2].ap(), lhsT,
                                         self.Ew.ap()[:, i * 512:(i + 1) * 512],
                                         start=(i == 0), stop=(i == 7))
                        if i == 7:
                            c.bump(inst, "tT", 1)
                    c.rec("vmm_%d" % tt, "tT")
                c.wtag(tn, "wkT")
                for kidx in range(32):
                    pt, gc = kidx // 8, kidx % 8
                    if kidx >= 2:
                        c.wtag(tn, "kev_%d" % (kidx - 2))
                    else:
                        c.wtag(tn, "vev_%d" % (30 + kidx))
                    for i in range(8):
                        lhsT = self.dstg(0, i * 512 + pt * 128, 128)
                        inst = tn.matmul(self.pb[kidx % 2].ap(), lhsT,
                                         self.xnT(i, gc * 512, 512),
                                         start=(i == 0), stop=(i == 7))
                        if i == 7:
                            c.bump(inst, "tT", 1)
                    c.rec("kmm_%d" % kidx, "tT")
                c.wtag(tn, "wqT")
                for qidx in range(32):
                    pt, gc = qidx // 8, qidx % 8
                    if qidx >= 2:
                        c.wtag(tn, "qev_%d" % (qidx - 2))
                    else:
                        c.wtag(tn, "kev_%d" % (30 + qidx))
                    for i in range(8):
                        lhsT = self.Ew.ap()[:, i * 512 + pt * 128:
                                            i * 512 + (pt + 1) * 128]
                        inst = tn.matmul(self.pb[qidx % 2].ap(), lhsT,
                                         self.xnT(i, gc * 512, 512),
                                         start=(i == 0), stop=(i == 7))
                        if i == 7:
                            c.bump(inst, "tT", 1)
                    c.rec("qmm_%d" % qidx, "tT")

            def fvec(v):
                c.wtag(v, "vones")
                for tt in range(32):
                    c.wtag(v, "vmm_%d" % tt)
                    src = self.pb[tt % 2].ap().rearrange("p (h c) -> p h c", h=NH)
                    dst = self.vsl(tt).rearrange("p (h c) -> p h c", c=65)
                    c.bump(v.tensor_copy(out=dst[:, :, 0:64], in_=src), "vV", 1)
                    c.rec("vev_%d" % tt, "vV")
                for kidx in range(32):
                    pt, gc = kidx // 8, kidx % 8
                    c.wtag(v, "kmm_%d" % kidx)
                    c.bump(v.tensor_copy(out=self.ksl(pt, gc * 512, 512),
                                         in_=self.pb[kidx % 2].ap()), "vV", 1)
                    c.rec("kev_%d" % kidx, "vV")
                for qidx in range(32):
                    pt, gc = qidx // 8, qidx % 8
                    c.wtag(v, "qmm_%d" % qidx)
                    c.bump(v.tensor_copy(out=self.qsl(pt, gc * 512, 512),
                                         in_=self.pb[qidx % 2].ap()), "vV", 1)
                    c.rec("qev_%d" % qidx, "vV")

            blk.sync(fsync)
            blk.gpsimd(fgps)
            blk.tensor(ftens)
            blk.vector(fvec)

        with nc.Block() as blk:
            def fsync(sy):
                for vt in range(4):
                    for mc in range(2):
                        c.bump(sy.dma_start(
                            out=self.C.ap()[:, 16384 + vt * 1024 + mc * 512:
                                            16384 + vt * 1024 + (mc + 1) * 512],
                            in_=self.wabo[mc * 512:(mc + 1) * 512,
                                          vt * 128:(vt + 1) * 128],
                            transpose=True), "dW0", 16)
                c.rec("woT", "dW0")

            def ftens(tn):
                last_linv = None
                last_ctx = [None, None]
                for pt in range(4):
                    for qc in range(8):
                        nkt = 4 * (qc + 1)
                        for kt in range(nkt + 1):
                            if kt < nkt:
                                sb = kt % 2
                                if kt >= 2:
                                    c.wtag(tn, "exp_%d_%d_%d_1" % (pt, qc, kt - 2))
                                lK = self.ksl(pt, kt * 128, 128)
                                rQ = self.qsl(pt, qc * 512, 512)
                                c.bump(tn.matmul(self.pb[sb * 2].ap(), lK[0:64, :],
                                                 rQ[0:64, :], start=True, stop=True,
                                                 tile_position=(0, 0)), "tT", 1)
                                c.rec("sc_%d_%d_%d_0" % (pt, qc, kt), "tT")
                                c.bump(tn.matmul(self.pb[sb * 2 + 1].ap(),
                                                 lK[64:128, :],
                                                 rQ[64:128, :], start=True, stop=True,
                                                 tile_position=(64, 0)), "tT", 1)
                                c.rec("sc_%d_%d_%d_1" % (pt, qc, kt), "tT")
                            if kt >= 1:
                                ka = kt - 1
                                sb = ka % 2
                                diag = (ka // 4 == qc)
                                for hd in range(2):
                                    if diag:
                                        c.wtag(tn, "expm_%d_%d_%d_%d"
                                               % (pt, qc, ka, hd))
                                    else:
                                        c.wtag(tn, "exp_%d_%d_%d_%d"
                                               % (pt, qc, ka, hd))
                                    if ka == 0 and last_ctx[hd] is not None:
                                        c.wtag(tn, last_ctx[hd])
                                    h = pt * 2 + hd
                                    eslot = self.expb.ap()[:, (sb * 2 + hd) * 512:
                                                           (sb * 2 + hd + 1) * 512]
                                    inst = tn.matmul(self.pb[4 + hd].ap()[0:65, :],
                                                     self.vsl(ka, h), eslot,
                                                     start=(ka == 0),
                                                     stop=(ka == nkt - 1))
                                    c.bump(inst, "tT", 1)
                                    c.rec("av_%d_%d_%d_%d" % (pt, qc, ka, hd), "tT")
                        for hd in range(2):
                            c.wtag(tn, "lrow_%d_%d_%d" % (pt, qc, hd))
                            if last_linv is not None:
                                c.wtag(tn, last_linv)
                            c.bump(tn.matmul(self.pb[6].ap()[0:64, :],
                                             self.onesb.ap()[64:65, 0:64],
                                             self.lrow_ap()[64:65, :],
                                             start=True, stop=True,
                                             tile_position=(64, 0)), "tT", 1)
                            c.rec("lbc_%d_%d_%d" % (pt, qc, hd), "tT")
                            nxt = (pt, qc, 1) if hd == 0 else \
                                ((pt, qc + 1, 0) if qc < 7 else (pt + 1, 0, 0))
                            self.ev["lbc_prev_%d_%d_%d" % nxt] = \
                                self.ev["lbc_%d_%d_%d" % (pt, qc, hd)]
                            last_linv = "linv_%d_%d_%d" % (pt, qc, hd)
                            last_ctx[hd] = "ctx_%d_%d_%d" % (pt, qc, hd)

            def fscal(s):
                for pt in range(4):
                    for qc in range(8):
                        nkt = 4 * (qc + 1)
                        for kt in range(nkt):
                            sb = kt % 2
                            c.wtag(s, "sc_%d_%d_%d_1" % (pt, qc, kt))
                            if kt >= 2:
                                c.wtag(s, "av_%d_%d_%d_1" % (pt, qc, kt - 2))
                            eslot = self.expb.ap()[:, sb * 1024:(sb + 1) * 1024]
                            psrc = (self.pbt01 if sb == 0 else self.pbt23).ap()
                            c.bump(s.activation(
                                out=eslot, in_=psrc,
                                func=AF.Exp, scale=c.slot("c_exp8")), "sS", 1)
                            c.rec("exp_%d_%d_%d_0" % (pt, qc, kt), "sS")
                            c.rec("exp_%d_%d_%d_1" % (pt, qc, kt), "sS")
                        for hd in range(2):
                            c.wtag(s, "av_%d_%d_%d_%d" % (pt, qc, nkt - 1, hd))
                            if not (pt == 0 and qc == 0 and hd == 0):
                                c.wtag(s, "lbc_prev_%d_%d_%d" % (pt, qc, hd))
                            lnrow = self.linv_ap()[64:65, :]
                            c.bump(s.activation(out=lnrow,
                                                in_=self.pb[4 + hd].ap()[64:65, :],
                                                func=AF.Ln), "sS", 1)
                            c.selfsem(s, "sS")
                            c.bump(s.activation(out=self.lrow_ap()[64:65, :],
                                                in_=lnrow, func=AF.Exp,
                                                scale=c.slot("c_neg1")[64:65, :]),
                                   "sS", 1)
                            c.rec("lrow_%d_%d_%d" % (pt, qc, hd), "sS")

            def fvec(v):
                for pt in range(4):
                    for qc in range(8):
                        nkt = 4 * (qc + 1)
                        for kt in range(nkt):
                            sb = kt % 2
                            if kt // 4 == qc:
                                ks = kt % 4
                                for hd in range(2):
                                    c.wtag(v, "exp_%d_%d_%d_%d" % (pt, qc, kt, hd))
                                    eslot = self.expb.ap()[:, (sb * 2 + hd) * 512:
                                                           (sb * 2 + hd + 1) * 512]
                                    c.bump(v.tensor_tensor(
                                        out=eslot, in0=eslot,
                                        in1=self.masks.ap()[:, ks * 512:
                                                            (ks + 1) * 512],
                                        op=ALU.mult), "vV", 1)
                                    c.rec("expm_%d_%d_%d_%d" % (pt, qc, kt, hd),
                                          "vV")
                        for hd in range(2):
                            c.wtag(v, "lbc_%d_%d_%d" % (pt, qc, hd))
                            c.bump(v.tensor_copy(out=self.linv_ap()[0:64, :],
                                                 in_=self.pb[6].ap()[0:64, :]),
                                   "vV", 1)
                            c.selfsem(v, "vV")
                            c.rec("linv_%d_%d_%d" % (pt, qc, hd), "vV")
                            dst = self.ctxsl(pt, qc * 512, 512)
                            dst = dst[hd * 64:(hd + 1) * 64, :]
                            c.bump(v.tensor_tensor(out=dst,
                                                   in0=self.pb[4 + hd].ap()[0:64, :],
                                                   in1=self.linv_ap()[0:64, :],
                                                   op=ALU.mult), "vV", 1)
                            c.rec("ctx_%d_%d_%d" % (pt, qc, hd), "vV")

            blk.sync(fsync)
            blk.tensor(ftens)
            blk.scalar(fscal)
            blk.vector(fvec)

        with nc.Block() as blk:
            Bf = self.B.ap().bitcast(F32)
            B4 = Bf.rearrange("p (b j m) -> p b j m", b=2, j=4)

            def ftens(tn):
                c.wtag(tn, "woT")
                for qc in range(8):
                    for dt in range(8):
                        gidx = qc * 8 + dt
                        if gidx >= 2:
                            c.wtag(tn, "opev_%d" % (gidx - 2))
                        for vt in range(4):
                            lhsT = self.wosl(vt, dt * 128, 128)
                            inst = tn.matmul(self.pb[gidx % 2].ap(), lhsT,
                                             self.ctxsl(vt, qc * 512, 512),
                                             start=(vt == 0), stop=(vt == 3))
                            if vt == 3:
                                c.bump(inst, "tT", 1)
                        c.rec("opmm_%d" % gidx, "tT")
                        if gidx >= 1:
                            self._emit_optr(tn, gidx - 1)
                self._emit_optr(tn, 63)

            def fscal(s):
                for gidx in range(64):
                    c.wtag(s, "opmm_%d" % gidx)
                    if gidx >= 2:
                        c.wtag(s, "optr_%d" % (gidx - 2))
                    c.bump(s.activation(
                        out=self.oT.ap()[:, (gidx % 2) * 512:(gidx % 2) * 512 + 512],
                        in_=self.pb[gidx % 2].ap(), func=AF.Copy), "sS", 1)
                    c.rec("opev_%d" % gidx, "sS")

            def fvec(v):
                for gidx in range(64):
                    qc, dt = gidx // 8, gidx % 8
                    c.wtag(v, "optr_%d" % gidx)
                    if dt == 0 and qc >= 2:
                        c.wtag(v, "rsst_%d" % (qc - 2))
                    pbt = self.pb[6 + gidx % 2].ap()
                    pbt = pbt.rearrange("p (j m) -> p j m", j=4)
                    c.bump(v.tensor_copy(
                        out=B4[:, qc % 2, :, dt * 128:(dt + 1) * 128], in_=pbt),
                        "vV", 1)
                    c.rec("opcp_%d" % gidx, "vV")
                c.wtag(v, "xrl_all")
                for t in range(NT):
                    c.wtag(v, "rsld_%d" % t)
                    buf = (self.stg1.ap() if t % 2 == 0
                           else self.expb.ap().bitcast(F32))
                    inst = v.tensor_tensor(out=c.xt(t), in0=c.xt(t), in1=buf,
                                           op=ALU.add)
                    c.bump(inst, "vV", 1)
                    c.rec("ra_%d" % t, "vV")

            def fgps(g):
                for qc in range(8):
                    c.wtag(g, "opcp_%d" % (qc * 8 + 7))
                    for j in range(4):
                        c.bump(g.dma_start(
                            out=self.rs_i.ap()[qc * 512 + j * 128:
                                               qc * 512 + (j + 1) * 128, :],
                            in_=Bf[:, (qc % 2) * 4096 + j * 1024:
                                   (qc % 2) * 4096 + (j + 1) * 1024]),
                            "dR%d" % (qc % 2), 16)
                    c.rec("rsst_%d" % qc, "dR%d" % (qc % 2))
                c.wall(g, "dR0")
                c.wall(g, "dR1")
                c.wall(g, "dG")
                c.bump(g.collective_compute(
                    "ReduceScatter", ALU.add,
                    replica_groups=[[0, 1], [2, 3], [4, 5], [6, 7]],
                    ins=[self.rs_i.ap().opt()], outs=[self.rs_o.ap().opt()]),
                    "cC", 1)
                c.wall(g, "cC")
                for t in range(NT):
                    buf = (self.stg1.ap() if t % 2 == 0
                           else self.expb.ap().bitcast(F32))
                    if t >= 2:
                        c.wtag(g, "ra_%d" % (t - 2))
                    c.bump(g.dma_start(out=buf,
                                       in_=self.rs_o.ap()[t * 128:(t + 1) * 128, :]),
                           "dR%d" % (t % 2), 16)
                    c.rec("rsld_%d" % t, "dR%d" % (t % 2))

            def fsync(sy):
                c.wtag(sy, "xspill")
                for t in range(NT):
                    c.bump(sy.dma_start(out=c.xt(t),
                                        in_=self.x_sp[t * 128:(t + 1) * 128, :]),
                           "dX", 16)
                c.rec("xrl_all", "dX")

            blk.sync(fsync)
            blk.tensor(ftens)
            blk.scalar(fscal)
            blk.vector(fvec)
            blk.gpsimd(fgps)

    def _emit_optr(self, tn, gidx):
        c = self
        c.wtag(tn, "opev_%d" % gidx)
        if gidx >= 2:
            c.wtag(tn, "opcp_%d" % (gidx - 2))
        src = self.oT.ap()[:, (gidx % 2) * 512:(gidx % 2) * 512 + 512]
        dstp = self.pb[6 + gidx % 2].ap()
        for j in range(4):
            inst = tn.transpose(dstp[:, j * 128:(j + 1) * 128],
                                src[:, j * 128:(j + 1) * 128], self.identf.ap())
            if j == 3:
                c.bump(inst, "tT", 1)
        c.rec("optr_%d" % gidx, "tT")

    def ph_store(self):
        c = self

        def fsync(sy):
            for t in range(NT):
                c.bump(sy.dma_start(out=c.y_out[t * 128:(t + 1) * 128, :],
                                    in_=c.xt(t)), "dS", 16)
            c.wall(sy, "dS")

        with self.nc.Block() as blk:
            blk.sync(fsync)


def build_program(parts=("ffn1", "attn", "ffn2"), dbg=()):
    mk1 = MK(parts, dbg=dbg)
    mk1.build()
    mk2 = MK(parts, plan=mk1.ev, dbg=dbg)
    return mk2.build()


def kernel(**inputs):
    x = np.ascontiguousarray(np.asarray(inputs["L0_attn_in_features"], np.float32))
    wq = np.asarray(inputs["L0_attn_q_proj_weight"], np.float32)
    wk = np.asarray(inputs["L0_attn_k_proj_weight"], np.float32)
    wv = np.asarray(inputs["L0_attn_v_proj_weight"], np.float32)
    wo = np.asarray(inputs["L0_attn_o_proj_weight"], np.float32)
    w1 = np.asarray(inputs["L0_ffn_W1"], np.float32)
    b1 = np.asarray(inputs["L0_ffn_b1"], np.float32)
    w2 = np.asarray(inputs["L0_ffn_W2"], np.float32)
    b2 = np.asarray(inputs["L0_ffn_b2"], np.float32)

    nc = build_program()
    in_maps = []
    for cid in range(NC):
        b, r = cid // 2, cid % 2
        hs = slice(r * NH * DH, (r + 1) * NH * DH)
        in_maps.append({
            "x": np.ascontiguousarray(x[b, r * T:(r + 1) * T, :]),
            "w1": w1, "b1": b1, "w2": w2, "b2": b2,
            "wqm": np.ascontiguousarray(wq[hs, :]),
            "wkm": np.ascontiguousarray(wk[hs, :]),
            "wvm": np.ascontiguousarray(wv[hs, :]),
            "wom": np.ascontiguousarray(wo[:, hs]),
        })
    res = run_bass_kernel_spmd(nc, in_maps, core_ids=list(range(NC)))
    out = np.empty((4, S, D), np.float32)
    for cid in range(NC):
        b, r = cid // 2, cid % 2
        out[b, r * T:(r + 1) * T, :] = res.results[cid]["y"]
    return out
